# version 14
# baseline (speedup 1.0000x reference)
import numpy as np
import concourse.bass as bass
import concourse.mybir as mybir
from concourse.bass_utils import run_bass_kernel_spmd

F32 = mybir.dt.float32
BF16 = mybir.dt.bfloat16
AX = mybir.AxisListType.X
OP = mybir.AluOpType
AF = mybir.ActivationFunctionType

STAGE = 4
FFN_PARTS = 4
FFN_NEXP = 8
FFN_B2N = 4
FFN_NG = 16
DBG = 0

NT = 4096
G = 32
GH = 16
NTIL = 8
DM = 1024
DS = 128
NE = 8


class Sem:
    def __init__(self, nc, name):
        self.h = nc.alloc_semaphore(name)
        self.n = 0


class B:

    def __init__(self):
        self.ops = []
        self.auto = {}

    def op(self, eng, fn, waits=(), inc=None):
        w = [(s, v) for (s, v) in waits if v is not None and v > 0]
        if eng in self.auto:
            sem = self.auto[eng]
            if sem.n > 0:
                w.append((sem, sem.n))
            if inc is None:
                inc = (sem, 1)
        if inc is not None:
            s, d = inc
            s.n += d
        self.ops.append((eng, fn, w, inc))
        return inc[0].n if inc else None

    def emit(self, eng_name, eng):
        maxw = {}
        for (e, fn, waits, inc) in self.ops:
            if e != eng_name:
                continue
            for (s, v) in waits:
                k = id(s)
                if maxw.get(k, -1) >= v:
                    continue
                maxw[k] = v
                eng.wait_ge(s.h, v)
            inst = fn(eng)
            if inc is not None:
                inst.then_inc(inc[0].h, inc[1])


def build_program(stage=None):
    if stage is None:
        stage = STAGE
    nc = bass.Bass("TRN2", target_bir_lowering=False)
    ap = lambda t: t.ap()

    x_d = ap(nc.dram_tensor("x", [NT, DM], F32, kind="ExternalInput"))
    W_in_d = ap(nc.dram_tensor("W_in", [2 * DM, DM], F32, kind="ExternalInput"))
    W_state_d = ap(nc.dram_tensor("W_state", [DS, DM], F32, kind="ExternalInput"))
    W_dt_d = ap(nc.dram_tensor("W_dt", [DS, DM], F32, kind="ExternalInput"))
    b_dt_d = ap(nc.dram_tensor("b_dt", [DS], F32, kind="ExternalInput"))
    A_log_d = ap(nc.dram_tensor("A_log", [DS], F32, kind="ExternalInput"))
    W_C_d = ap(nc.dram_tensor("W_C", [DM, DS], F32, kind="ExternalInput"))
    D_d = ap(nc.dram_tensor("Dvec", [DM], F32, kind="ExternalInput"))
    W_outs_d = ap(nc.dram_tensor("W_out_ssm", [DM, DM], F32, kind="ExternalInput"))
    w_rt_d = ap(nc.dram_tensor("w_router", [DM, NE], F32, kind="ExternalInput"))
    b_rt_d = ap(nc.dram_tensor("b_router", [NE], F32, kind="ExternalInput"))
    w_exp_d = ap(nc.dram_tensor("w_experts", [NE, DM, DM], F32, kind="ExternalInput"))
    b_exp_d = ap(nc.dram_tensor("b_experts", [NE, DM], F32, kind="ExternalInput"))
    b_expb_d = ap(nc.dram_tensor("b_exp_bcast", [NE, 128, DM], F32, kind="ExternalInput"))
    w_out_d = ap(nc.dram_tensor("w_out_ffn", [DM, DM], F32, kind="ExternalInput"))
    b_out_d = ap(nc.dram_tensor("b_out", [DM], F32, kind="ExternalInput"))
    ident_d = ap(nc.dram_tensor("ident", [128, 128], BF16, kind="ExternalInput"))
    hsel_d = ap(nc.dram_tensor("hsel", [NE, 1], F32, kind="ExternalInput"))
    out_d = ap(nc.dram_tensor("out", [NT, DM], F32, kind="ExternalOutput"))
    cc_in_d = ap(nc.dram_tensor("cc_in", [1, DS], F32))
    cc_out_d = ap(nc.dram_tensor("cc_out", [NE, DS], F32))

    al = lambda *a: nc.alloc_sbuf_tensor(*a).ap()

    resid = al("resid", [128, G * DM], BF16)
    ident = al("identsb", [128, 128], BF16)
    ones1 = al("ones1", [1, 128], BF16)
    w_rt = al("w_rt", [128, 8 * NE], BF16)
    b_rt = al("b_rt", [1, NE], BF16)
    b_out = al("b_outsb", [1, DM], BF16)
    D_sb = al("D_sb", [128, 8], F32)
    b_dt = al("b_dtsb", [128, 1], F32)
    nrate = al("nrate", [128, 1], F32)
    hsel = al("hselsb", [NE, 1], F32)
    hall = al("hallsb", [NE, DS], F32)
    H_in = al("H_insb", [128, 1], F32)
    negmean = al("negmean", [128, G], F32)
    rstd = al("rstd", [128, G], F32)
    varn = al("varn", [128, G], F32)
    logits = al("logits", [128, GH * NE], F32)
    masks1 = al("masks1", [128, GH * NE], F32)
    lg2 = al("lg2", [128, GH * NE], F32)
    wts = al("wts", [128, GH * NE], F32)
    m1s = al("m1s", [128, GH], F32)
    m2s = al("m2s", [128, GH], F32)
    svals = al("svals", [128, GH], F32)
    scr = al("scr", [128, 8], F32)
    xlt = al("xlt", [128, 2 * DM], BF16)
    out_sb = al("out_sbuf", [128, DM], F32)
    b_bc = al("b_bc", [128, 2 * DM], BF16)

    pe0 = nc.alloc_psum_tensor("pe0", [128, 1024], F32).ap()
    pe1 = nc.alloc_psum_tensor("pe1", [128, 1024], F32).ap()
    pw = nc.alloc_psum_tensor("pw", [128, 512], F32).ap()
    pr = nc.alloc_psum_tensor("pr", [128, 512], F32).ap()
    ptA = nc.alloc_psum_tensor("ptA", [128, 1024], BF16).ap()
    ptB = nc.alloc_psum_tensor("ptB", [128, 1024], BF16).ap()

    from contextlib import ExitStack
    with ExitStack() as st_f:
        e = st_f.enter_context
        x_lnT = e(nc.sbuf_tensor("x_lnT", [128, GH * DM], BF16)).ap()
        Gb = e(nc.sbuf_tensor("Gb", [128, 2 * DM], BF16)).ap()
        combT = e(nc.sbuf_tensor("combT", [128, 2 * DM], BF16)).ap()
        expb = e(nc.sbuf_tensor("expb", [128, 2 * 8 * DM], BF16)).ap()
        acc = e(nc.sbuf_tensor("acc", [128, GH * DM], BF16)).ap()
        w_out = e(nc.sbuf_tensor("w_out", [128, 8 * DM], BF16)).ap()
    with ExitStack() as st_s:
        e = st_s.enter_context
        W_inT = e(nc.sbuf_tensor("W_inT", [128, 8 * DM], BF16)).ap()
        W_oT = e(nc.sbuf_tensor("W_oT", [128, 8 * DM], BF16)).ap()
        W_sT = e(nc.sbuf_tensor("W_sT", [128, 8 * DS], BF16)).ap()
        W_dT = e(nc.sbuf_tensor("W_dT", [128, 8 * DS], BF16)).ap()
        W_cT = e(nc.sbuf_tensor("W_cT", [128, DM], BF16)).ap()
        _pad = e(nc.sbuf_tensor("_pad", [128, DM], BF16)).ap()
        decay = e(nc.sbuf_tensor("decay", [128, NT], F32)).ap()
        xsc = e(nc.sbuf_tensor("xsc", [128, NT], F32)).ap()
        hs = e(nc.sbuf_tensor("hs", [128, NT], F32)).ap()
        hs_bf = e(nc.sbuf_tensor("hs_bf", [128, 512], BF16)).ap()
        xlTt = e(nc.sbuf_tensor("xlTt", [128, NT], BF16)).ap()
        u_bf = e(nc.sbuf_tensor("u_bf", [128, NT], BF16)).ap()
        z_bf = e(nc.sbuf_tensor("z_bf", [128, NT], BF16)).ap()
        sg_b = e(nc.sbuf_tensor("sg_b", [128, 2 * 512], BF16)).ap()
        dts = e(nc.sbuf_tensor("dts", [128, 512], F32)).ap()
        zt = e(nc.sbuf_tensor("zt", [128, 512], F32)).ap()
        stage_b = e(nc.sbuf_tensor("stage_b", [128, 2 * DM], BF16)).ap()

    b = B()
    sdw = Sem(nc, "sdw")
    sx = [Sem(nc, f"sx{q}") for q in range(8)]
    sexp = [Sem(nc, f"sexp{e}") for e in range(8)]
    sprep = Sem(nc, "sprep")
    sprep2 = Sem(nc, "sprep2")
    swout = Sem(nc, "swout")
    shall = Sem(nc, "shall")
    sccin = Sem(nc, "sccin")
    sdo2 = [Sem(nc, "sdo0"), Sem(nc, "sdo1")]
    sdo = Sem(nc, "sdo")
    scc = Sem(nc, "scc")
    sT = Sem(nc, "sT")
    sS = Sem(nc, "sS")
    sV = Sem(nc, "sV")
    sG = Sem(nc, "sG")
    sY = Sem(nc, "sY")
    ENG = {"tensor": sT, "scalar": sS, "vector": sV, "gpsimd": sG, "sync": sY}
    b.auto = {"vector": sV, "scalar": sS}

    def fence():
        return [(sT, sT.n), (sS, sS.n), (sV, sV.n)]

    def gdma(out, in_, waits=(), slow=False, sem=None):
        sem = sem if sem is not None else sdw
        return b.op("gpsimd", lambda g, o=out, i=in_, sl=slow: g.dma_start(
            out=o, in_=i, allow_slow_non_contiguous=sl), waits, (sem, 16))

    def ydma(out, in_, waits=(), sem=None):
        sem = sem if sem is not None else sdo
        return b.op("sync", lambda s, o=out, i=in_: s.dma_start(out=o, in_=i),
                    waits, (sem, 16))



    gdma(ident, ident_d[:, :])
    gdma(hsel, hsel_d[:, :])
    gdma(w_rt.rearrange("p (k e) -> p k e", e=NE),
         w_rt_d.rearrange("(k p) e -> p k e", p=128))
    gdma(b_rt, b_rt_d.unsqueeze(0))
    gdma(b_out, b_out_d.unsqueeze(0))
    gdma(D_sb, D_d.rearrange("(m p) -> p m", p=128), slow=True)
    gdma(b_dt, b_dt_d.unsqueeze(1))
    w_small = gdma(nrate, A_log_d.unsqueeze(1))
    X_SPANS = [(0, 1), (1, 1), (2, 1), (3, 1), (4, 2), (6, 2), (8, 4), (12, 4),
               (16, 16)]
    x_wait = {}
    for ci, (off, ngr) in enumerate(X_SPANS):
        sem = sx[ci] if ci < 8 else sdo
        v = gdma(resid[:, off * DM:(off + ngr) * DM].rearrange("p (g d) -> p g d", d=DM),
                 x_d[off * 128:(off + ngr) * 128, :].rearrange("(g p) d -> p g d", p=128),
                 sem=sem)
        for g in range(off, off + ngr):
            x_wait[g] = (sem, v)

    ones_v = b.op("vector", lambda v: v.memset(ones1, 1.0), (), (sV, 1))
    v0 = b.op("scalar", lambda s: s.activation(scr[:, 0:1], nrate, AF.Exp),
              [(sdw, w_small)], (sS, 1))
    v = b.op("scalar", lambda s: s.activation(scr[:, 0:1], scr[:, 0:1], AF.Ln, bias=1.0),
             [(sS, v0)], (sS, 1))
    nrate_v = b.op("vector", lambda v2: v2.tensor_scalar(nrate, scr[:, 0:1], 1e-4, -1.0,
                                                         OP.add, OP.mult), [(sS, v)], (sV, 1))

    xlt_guard = {0: None, 1: None}

    def ln_stats_raw(groups, first_waits=()):
        for i, g in enumerate(groups):
            rs = resid[:, g * DM:(g + 1) * DM]
            b.op("vector", lambda v, rs=rs, g=g: v.reduce_sum(
                negmean[:, g:g + 1], rs, AX, negate=True),
                [x_wait[g]] + (list(first_waits) if i == 0 else []))
        f = b.op("vector", lambda v, g0=groups[0], n=len(groups): v.tensor_scalar(
            negmean[:, g0:g0 + n], negmean[:, g0:g0 + n], 1.0 / DM, None, OP.mult),
            (), (sV, 1))
        for i, g in enumerate(groups):
            rs = resid[:, g * DM:(g + 1) * DM]
            b.op("vector", lambda v, rs=rs, g=g: v.scalar_tensor_tensor(
                out_sb[:, 0:DM], rs, negmean[:, g:g + 1], rs, OP.add, OP.mult,
                accum_out=varn[:, g:g + 1]), [(sV, f)] if i == 0 else ())

    def ln_stats_fin(g0, n):
        b.op("vector", lambda v: v.tensor_scalar(
            varn[:, g0:g0 + n], varn[:, g0:g0 + n], 1.0 / DM, 1e-5, OP.mult, OP.add))
        f2 = b.op("vector", lambda v: v.reciprocal(
            rstd[:, g0:g0 + n], varn[:, g0:g0 + n]), (), (sV, 1))
        f3 = b.op("scalar", lambda s: s.activation(
            rstd[:, g0:g0 + n], rstd[:, g0:g0 + n], AF.Sqrt), [(sV, f2)], (sS, 1))
        return (sS, f3)

    def ln_stats(groups, g0, first_waits=()):
        ln_stats_raw(groups, first_waits)
        return ln_stats_fin(g0, len(groups))

    def ln_stats_thunks(groups, g0):
        th = []
        box = {}
        for g in groups:
            th.append(lambda g=g: b.op("vector", lambda v, g=g: v.reduce_sum(
                negmean[:, g:g + 1], resid[:, g * DM:(g + 1) * DM], AX,
                negate=True), [x_wait[g]]))
        th.append(lambda g0=g0, n=len(groups): b.op(
            "vector", lambda v: v.tensor_scalar(
                negmean[:, g0:g0 + n], negmean[:, g0:g0 + n], 1.0 / DM, None,
                OP.mult)))
        for g in groups:
            th.append(lambda g=g: b.op("vector", lambda v, g=g: v.scalar_tensor_tensor(
                out_sb[:, 0:DM], resid[:, g * DM:(g + 1) * DM],
                negmean[:, g:g + 1], resid[:, g * DM:(g + 1) * DM],
                OP.add, OP.mult, accum_out=varn[:, g:g + 1])))

        def t_fin(g0=g0, n=len(groups)):
            box["h"] = ln_stats_fin(g0, n)
        th.append(t_fin)
        return th, box

    xln_prev = {"n": None}

    def x_ln_group(g, slot, stats_ready, resid_dep=None):
        rs = resid[:, g * DM:(g + 1) * DM]
        dst = xlt[:, slot * DM:(slot + 1) * DM]
        w = [stats_ready]
        if xlt_guard[slot] is not None:
            w.append((sT, xlt_guard[slot]))
        if xln_prev["n"] is not None:
            w.append((sG, xln_prev["n"]))
        if resid_dep is not None:
            w.append(resid_dep)
        r = b.op("gpsimd", lambda gg, rs=rs, dst=dst, g=g: gg.tensor_scalar(
            dst, rs, negmean[:, g:g + 1], rstd[:, g:g + 1], OP.add, OP.mult), w, (sG, 1))
        xln_prev["n"] = r
        return r

    pt_rot = {"i": 0, "free": [None, None], "banks": [ptA, ptB]}

    def transp8(src, xr_wait, slot=None, evac_eng="scalar"):
        i = pt_rot["i"] % 2
        pt_rot["i"] += 1
        pt = pt_rot["banks"][i]
        t_last = None
        for k in range(8):
            w = []
            if k == 0:
                w = [xr_wait]
                if pt_rot["free"][i] is not None:
                    w.append(pt_rot["free"][i])
                w = [x for x in w if x is not None]
            inc = (sT, 1) if k == 7 else None
            t_last = b.op("tensor", lambda t, k=k, src=src, pt=pt: t.transpose(
                pt[:, k * 128:(k + 1) * 128], src[:, k * 128:(k + 1) * 128], ident), w, inc)
        if slot is not None:
            xlt_guard[slot] = t_last
        return pt, i, t_last

    def transp8_evac(dst, xr_wait, src, slot=None, eng="scalar", dst3d=None):
        pt, i, tlast = transp8(src, xr_wait, slot=slot)
        d = dst if dst3d is None else dst3d
        if eng == "scalar":
            ev = b.op("scalar", lambda s2, d=d, pt=pt: s2.copy(
                d, pt[:, 0:1024] if dst3d is None else pt.rearrange("p (k c) -> p k c", c=128)),
                [(sT, tlast)], (sS, 1))
            sem = sS
        else:
            ev = b.op("vector", lambda v2, d=d, pt=pt: v2.tensor_copy(
                d, pt[:, 0:1024] if dst3d is None else pt.rearrange("p (k c) -> p k c", c=128)),
                [(sT, tlast)], (sV, 1))
            sem = sV
        pt_rot["free"][i] = (sem, ev)
        return (sem, ev), tlast

    store_seq = []

    def ffn(call, exp_early=None, stats_in=None):
        final = (call == 1 and stage >= 4)
        prefetched = {}
        if exp_early is not None:
            for e_ in (0, 1):
                slot = e_ % 2
                dst = expb[:, slot * 8 * DM:(slot + 1) * 8 * DM].rearrange(
                    "p (k d) -> p k d", d=DM)
                src = w_exp_d[e_].rearrange("(k p) d -> p k d", p=128)
                gdma(dst, src, list(exp_early) + [(sdw, w_small)], sem=sexp[e_])
                prefetched[e_] = gdma(b_bc[:, slot * DM:(slot + 1) * DM],
                                      b_expb_d[e_], sem=sexp[e_])
        wo_v = gdma(w_out.rearrange("p (k e) -> p k e", e=DM),
                    w_out_d.rearrange("(k p) e -> p k e", p=128), fence(), sem=swout)
        call_fence = fence()
        if stats_in is None:
            smap = {}
            pend_chunks = [sp for sp in X_SPANS if sp[0] < GH]
            bg_thunks = []
            bg_box = {}
            for (coff, cn) in [sp for sp in X_SPANS if sp[0] >= GH]:
                th, box = ln_stats_thunks(list(range(coff, coff + cn)), coff)
                bg_thunks += th
                for gg in range(coff, coff + cn):
                    bg_box[gg] = box
        else:
            smap = {g: stats_in for g in range(GH)}
            pend_chunks = []
            bg_thunks = []
            bg_box = {}
            th, box = ln_stats_thunks(list(range(GH, G)), GH)
            bg_thunks += th
            for gg in range(GH, G):
                bg_box[gg] = box
        prpw_free = {0: None, 1: None}
        exp_last_mm = {}
        load_vals = {}
        badd_last = {}
        exp_use = {}
        for half in range(2):
            half_fence = fence()
            if half == 1:
                while bg_thunks:
                    bg_thunks.pop(0)()
                for gg, box in bg_box.items():
                    smap[gg] = box["h"]
            groups = list(range(half * GH, (half + 1) * GH))[:FFN_NG]
            ptA_free = None
            prev_rt_evac = None
            xlnT_ev = {}
            pend_b2 = None

            def b2_tail(pt, i, tlast, gl, g):
                dst = x_lnT[:, gl * DM:(gl + 1) * DM]
                ev = b.op("vector", lambda v, dst=dst, pt=pt: v.tensor_copy(
                    dst, pt[:, 0:1024]), [(sT, tlast)], (sV, 1))
                pt_rot["free"][i] = (sV, ev)
                xlnT_ev[g] = (sV, ev)

                def rt(t, gl=gl):
                    for k in range(8):
                        t.matmul(pr[:, 0:NE], x_lnT[:, (gl * 8 + k) * 128:(gl * 8 + k + 1) * 128],
                                 w_rt[:, k * NE:(k + 1) * NE], start=(k == 0), stop=False)
                    return t.matmul(pr[:, 0:NE], ones1, b_rt, start=False, stop=True)
                w = [xlnT_ev[g], (sdw, w_small), (sV, ones_v)]
                if b2_state["prev_rt"] is not None:
                    w.append((sS, b2_state["prev_rt"]))
                if prpw_free[1] is not None:
                    w.append(prpw_free[1])
                    prpw_free[1] = None
                tv = b.op("tensor", rt, w, (sT, 1))
                b2_state["prev_rt"] = b.op("scalar", lambda s, gl=gl: s.copy(
                    logits[:, gl * NE:(gl + 1) * NE], pr[:, 0:NE]), [(sT, tv)], (sS, 1))

            b2_state = {"prev_rt": prev_rt_evac}
            for j, g in enumerate(groups):
                gl = g - half * GH
                slot = j % 2
                while pend_chunks and pend_chunks[0][0] <= g:
                    coff, cn = pend_chunks.pop(0)
                    h = ln_stats(list(range(coff, coff + cn)), coff,
                                 call_fence if coff == 0 else ())
                    for gg in range(coff, coff + cn):
                        smap[gg] = h
                xr = x_ln_group(g, slot, smap[g])
                pt, i, tlast = transp8(xlt[:, slot * DM:(slot + 1) * DM], (sG, xr), slot=slot)
                if pend_b2 is not None:
                    b2_tail(*pend_b2)
                pend_b2 = (pt, i, tlast, gl, g)
            if pend_b2 is not None:
                b2_tail(*pend_b2)
                pend_b2 = None
            prev_rt_evac = b2_state["prev_rt"]
            if FFN_PARTS < 2:
                continue
            lg3 = logits.rearrange("p (g e) -> p g e", e=NE)
            m1f = b.op("vector", lambda v: v.tensor_reduce(m1s[:, 0:GH], lg3, AX, OP.max),
                       [(sS, prev_rt_evac)], (sV, 1))
            for j in range(GH):
                b.op("vector", lambda v, j=j: v.tensor_single_scalar(
                    masks1[:, j * NE:(j + 1) * NE], logits[:, j * NE:(j + 1) * NE],
                    m1s[:, j:j + 1], OP.is_ge), [(sV, m1f)] if j == 0 else ())
                b.op("vector", lambda v, j=j: v.scalar_tensor_tensor(
                    lg2[:, j * NE:(j + 1) * NE], masks1[:, j * NE:(j + 1) * NE], -1e30,
                    logits[:, j * NE:(j + 1) * NE], OP.mult, OP.add))
            lg23 = lg2.rearrange("p (g e) -> p g e", e=NE)
            b.op("vector", lambda v: v.tensor_reduce(m2s[:, 0:GH], lg23, AX, OP.max))
            dv = b.op("vector", lambda v: v.tensor_sub(svals[:, 0:GH], m1s[:, 0:GH],
                                                       m2s[:, 0:GH]), (), (sV, 1))
            sf = b.op("scalar", lambda s: s.activation(svals[:, 0:GH], svals[:, 0:GH],
                                                       AF.Sigmoid), [(sV, dv)], (sS, 1))
            for j in range(GH):
                b.op("vector", lambda v, j=j: v.tensor_single_scalar(
                    wts[:, j * NE:(j + 1) * NE], lg2[:, j * NE:(j + 1) * NE],
                    m2s[:, j:j + 1], OP.is_ge), [(sS, sf)] if j == 0 else ())
                b.op("vector", lambda v, j=j: v.scalar_tensor_tensor(
                    wts[:, j * NE:(j + 1) * NE], wts[:, j * NE:(j + 1) * NE],
                    svals[:, j:j + 1], wts[:, j * NE:(j + 1) * NE], OP.mult, OP.subtract))
                b.op("vector", lambda v, j=j: v.scalar_tensor_tensor(
                    wts[:, j * NE:(j + 1) * NE], masks1[:, j * NE:(j + 1) * NE],
                    svals[:, j:j + 1], wts[:, j * NE:(j + 1) * NE], OP.mult, OP.subtract))
            wtsf = b.op("vector", lambda v: v.tensor_copy(scr[:, 0:1], wts[:, 0:1]),
                        (), (sV, 1))
            if FFN_PARTS < 3:
                continue

            gelu_done = {}
            stt_done = {}
            badd_done = {}
            pend_stt = None

            def emit_stt(e_, g, gl, gslot, sv_):
                accg = acc[:, gl * DM:(gl + 1) * DM]
                if e_ == 0:
                    vv = b.op("vector", lambda v, accg=accg, gslot=gslot, gl=gl:
                              v.tensor_single_scalar(
                                  accg, Gb[:, gslot * DM:(gslot + 1) * DM],
                                  wts[:, gl * NE:gl * NE + 1], OP.mult),
                              [(sS, sv_), (sV, wtsf)], (sV, 1))
                else:
                    vv = b.op("vector", lambda v, accg=accg, gslot=gslot, gl=gl, e_=e_:
                              v.scalar_tensor_tensor(
                                  accg, Gb[:, gslot * DM:(gslot + 1) * DM],
                                  wts[:, gl * NE + e_:gl * NE + e_ + 1],
                                  accg, OP.mult, OP.add), [(sS, sv_)], (sV, 1))
                stt_done[(e_, g)] = vv
            for e_ in range(FFN_NEXP):
                slot = e_ % 2
                if half == 0 and e_ in prefetched:
                    load_vals[e_] = prefetched[e_]
                else:
                    w = [(sdw, w_small)]
                    if half == 0 and e_ < 2:
                        w += list(half_fence)
                    if e_ >= 2 or half == 1:
                        prev_e = e_ - 2 if e_ >= 2 else e_ + 6
                        w.append((sT, exp_last_mm[prev_e]))
                        w.append((sV, badd_last[prev_e]))
                    dst = expb[:, slot * 8 * DM:(slot + 1) * 8 * DM].rearrange(
                        "p (k d) -> p k d", d=DM)
                    src = w_exp_d[e_].rearrange("(k p) d -> p k d", p=128)
                    gdma(dst, src, w, sem=sexp[e_])
                    load_vals[e_] = gdma(
                        b_bc[:, slot * DM:(slot + 1) * DM], b_expb_d[e_], sem=sexp[e_])
                for j, g in enumerate(groups):
                    gl = g - half * GH
                    bank = pe0 if (j % 2 == 0) else pe1

                    def emm(t, gl=gl, bank=bank, slot=slot):
                        r = None
                        for k in range(8):
                            lhs = x_lnT[:, (gl * 8 + k) * 128:(gl * 8 + k + 1) * 128]
                            rhs0 = expb[:, slot * 8 * DM + k * DM:slot * 8 * DM + k * DM + 512]
                            rhs1 = expb[:, slot * 8 * DM + k * DM + 512:slot * 8 * DM + (k + 1) * DM]
                            t.matmul(bank[:, 0:512], lhs, rhs0, start=(k == 0), stop=(k == 7))
                            r = t.matmul(bank[:, 512:1024], lhs, rhs1, start=(k == 0), stop=(k == 7))
                        return r
                    w = [(sexp[e_], load_vals[e_]), xlnT_ev[g]]
                    if j >= 2:
                        w.append((sV, badd_done[(e_, groups[j - 2])]))
                    elif e_ > 0:
                        w.append((sV, badd_done[(e_ - 1, groups[GH - 2 + j])]))
                    tv = b.op("tensor", emm, w, (sT, 1))
                    exp_last_mm[e_] = tv
                    gslot = j % 2
                    wV = [(sT, tv)]
                    if j >= 2:
                        wV.append((sS, gelu_done[(e_, groups[j - 2])]))
                    elif e_ > 0:
                        wV.append((sS, gelu_done[(e_ - 1, groups[GH - 2 + j])]))
                    bv = b.op("vector", lambda v, bank=bank, gslot=gslot, slot=slot:
                              v.tensor_add(Gb[:, gslot * DM:(gslot + 1) * DM],
                                           bank[:, 0:1024],
                                           b_bc[:, slot * DM:(slot + 1) * DM]),
                              wV, (sV, 1))
                    badd_done[(e_, g)] = bv
                    badd_last[e_] = bv
                    wS = [(sV, bv)]
                    if j >= 2:
                        wS.append((sV, stt_done[(e_, groups[j - 2])]))
                    elif e_ > 0:
                        wS.append((sV, stt_done[(e_ - 1, groups[GH - 2 + j])]))
                    sv_ = b.op("scalar", lambda s, gslot=gslot: s.activation(
                        Gb[:, gslot * DM:(gslot + 1) * DM],
                        Gb[:, gslot * DM:(gslot + 1) * DM],
                        AF.Gelu_apprx_tanh), wS, (sS, 1))
                    gelu_done[(e_, g)] = sv_
                    if pend_stt is not None:
                        emit_stt(*pend_stt)
                    pend_stt = (e_, g, gl, gslot, sv_)
                    if e_ >= 4 and bg_thunks:
                        bg_thunks.pop(0)()

            if pend_stt is not None:
                emit_stt(*pend_stt)
                pend_stt = None
            if FFN_PARTS < 4:
                continue
            pend_b5 = None
            b5_state = {"wout_last": None}

            def b5_tail(ev, slot, j, g):
                pv = None
                for hh in range(2):
                    wb = (pw, 0) if (j * 2 + hh) % 2 == 0 else (pr, 1)
                    wbank, wbi = wb

                    def wmm(t, slot=slot, hh=hh, wbank=wbank):
                        t.matmul(wbank, ones1, b_out[:, hh * 512:(hh + 1) * 512],
                                 start=True, stop=False)
                        r = None
                        for k in range(8):
                            lhs = combT[:, slot * DM + k * 128:slot * DM + (k + 1) * 128]
                            rhs = w_out[:, k * DM + hh * 512:k * DM + hh * 512 + 512]
                            r = t.matmul(wbank, lhs, rhs, start=False, stop=(k == 7))
                        return r
                    w = [(sS, ev), (swout, wo_v), (sV, ones_v)]
                    if prpw_free[wbi] is not None:
                        w.append(prpw_free[wbi])
                    tv = b.op("tensor", wmm, w, (sT, 1))
                    b5_state["wout_last"] = tv
                    rs = resid[:, g * DM + hh * 512:g * DM + hh * 512 + 512]
                    if final:
                        dst = out_sb[:, hh * 512:hh * 512 + 512]
                        wV = [(sT, tv)]
                        if hh == 0 and len(store_seq) >= 1:
                            wV.append(store_seq[-1])
                        pv = b.op("vector", lambda v, dst=dst, rs=rs, wbank=wbank:
                                  v.scalar_tensor_tensor(dst, wbank, 0.5, rs,
                                                         OP.mult, OP.add), wV, (sV, 1))
                    else:
                        pv = b.op("vector", lambda v, rs=rs, wbank=wbank:
                                  v.scalar_tensor_tensor(rs, wbank, 0.5, rs,
                                                         OP.mult, OP.add), [(sT, tv)], (sV, 1))
                    prpw_free[wbi] = (sV, pv)
                if final:
                    sv2 = ydma(out_d[g * 128:(g + 1) * 128, :],
                               out_sb[:, 0:DM], [(sV, pv)], sem=sdo2[0])
                    store_seq.append((sdo2[0], sv2))

            for j, g in enumerate(groups):
                gl = g - half * GH
                slot = j % 2
                accg = acc[:, gl * DM:(gl + 1) * DM]
                ctd = combT[:, slot * DM:(slot + 1) * DM]
                pt, i, tlast = transp8(accg, (sV, stt_done[(FFN_NEXP - 1, g)]))
                wE = [(sT, tlast)]
                if b5_state["wout_last"] is not None and j >= 2:
                    wE.append((sT, b5_state["wout_last"]))
                ev = b.op("scalar", lambda s2, ctd=ctd, pt=pt: s2.copy(ctd, pt[:, 0:1024]),
                          wE, (sS, 1))
                pt_rot["free"][i] = (sS, ev)
                if pend_b5 is not None:
                    b5_tail(*pend_b5)
                pend_b5 = (ev, slot, j, g)
            if pend_b5 is not None:
                b5_tail(*pend_b5)
                pend_b5 = None

    def ssm():
        ssm_fence = fence()
        prev_t = {0: None, 1: None}

        def prep2(src_rows, n_rchunks, n_k, dstT, dst_cols_per_k, first_waits=()):
            last_t = None
            for r in range(n_rchunks):
                w = list(first_waits) if r == 0 else []
                if prev_t[r % 2] is not None:
                    w.append((sT, prev_t[r % 2]))
                ps_ = sprep if r % 2 == 0 else sprep2
                w.append((ps_, ps_.n))
                ld = gdma(stage_b[:, r % 2 * 1024:r % 2 * 1024 + n_k * 128],
                          src_rows[r * 128:(r + 1) * 128, :], w, sem=ps_)
                i = pt_rot["i"] % 2
                pt_rot["i"] += 1
                pt = pt_rot["banks"][i]
                for k in range(n_k):
                    ww = []
                    if k == 0:
                        ww = [(ps_, ld)]
                        if pt_rot["free"][i] is not None:
                            ww.append(pt_rot["free"][i])
                    inc = (sT, 1) if k == n_k - 1 else None
                    sb_off = (r % 2) * 1024
                    last_t = b.op("tensor", lambda t, k=k, pt=pt, sb_off=sb_off: t.transpose(
                        pt[:, k * 128:(k + 1) * 128],
                        stage_b[:, sb_off + k * 128:sb_off + (k + 1) * 128],
                        ident), ww, inc)
                dv2 = dstT.rearrange("p (k c) -> p k c", c=dst_cols_per_k)[
                    :, 0:n_k, r * 128:(r + 1) * 128]
                sv2 = pt.rearrange("p (k c) -> p k c", c=128)[:, 0:n_k, :]
                ev = b.op("scalar", lambda s, dv2=dv2, sv2=sv2: s.copy(dv2, sv2),
                          [(sT, last_t)], (sS, 1))
                pt_rot["free"][i] = (sS, ev)
                prev_t[r % 2] = last_t
            return ev

        prep2(W_in_d[0:DM], 8, 8, W_inT, 1024, ssm_fence)
        prep2(W_outs_d, 8, 8, W_oT, 1024)
        prep2(W_state_d, 1, 8, W_sT, 128)
        prep2(W_dt_d, 1, 8, W_dT, 128)
        prep_last = prep2(W_C_d, 8, 1, W_cT, 1024)

        stats_ready = ln_stats(list(range(G)), 0, ssm_fence)

        def build_xlnT_tile(til, resid_dep=None):
            ptf = None
            for gp in range(4):
                g = til * 4 + gp
                slot = gp % 2
                xr = x_ln_group(g, slot, stats_ready, resid_dep=resid_dep)
                dv2 = xlTt.rearrange("p (k c) -> p k c", c=512)[:, :, gp * 128:(gp + 1) * 128]
                (es, ev), _tl = transp8_evac(None, (sG, xr),
                                             xlt[:, slot * DM:(slot + 1) * DM],
                                             slot=slot, eng="scalar", dst3d=dv2)
                ptf = (es, ev)
            return ptf

        slot_evac = {0: None, 1: None, 2: None, 3: None}
        pw_free = None
        pr_free = None
        dec_s = None
        bt_v = None
        for til in range(NTIL):
            xl_done = build_xlnT_tile(til)
            tc = til * 512
            uslots = [(pe0, 0), (pe0, 512), (pe1, 0), (pe1, 512)]
            u_ready = None
            for m in range(8):
                bank, off = uslots[m % 4]

                def umm(t, m=m, bank=bank, off=off):
                    r = None
                    for k in range(8):
                        r = t.matmul(bank[:, off:off + 512],
                                     W_inT[:, k * DM + m * 128:k * DM + (m + 1) * 128],
                                     xlTt[:, k * 512:(k + 1) * 512],
                                     start=(k == 0), stop=(k == 7))
                    return r
                w = [xl_done]
                if slot_evac[m % 4] is not None:
                    w.append((sS, slot_evac[m % 4]))
                tv = b.op("tensor", umm, w, (sT, 1))
                ev = b.op("scalar", lambda s, m=m, bank=bank, off=off: s.copy(
                    u_bf[:, m * 512:(m + 1) * 512], bank[:, off:off + 512]),
                    [(sT, tv)], (sS, 1))
                slot_evac[m % 4] = ev
                u_ready = ev

            def btmm(t):
                r = None
                for k in range(8):
                    r = t.matmul(pw, W_sT[:, k * 128:(k + 1) * 128],
                                 u_bf[:, k * 512:(k + 1) * 512], start=(k == 0), stop=(k == 7))
                return r
            w = [(sS, u_ready)]
            if bt_v is not None:
                w.append((sV, bt_v))
            bt_t = b.op("tensor", btmm, w, (sT, 1))

            def dtmm(t):
                r = None
                for k in range(8):
                    r = t.matmul(pr, W_dT[:, k * 128:(k + 1) * 128],
                                 u_bf[:, k * 512:(k + 1) * 512], start=(k == 0), stop=(k == 7))
                return r
            w = [(sS, u_ready)]
            if pr_free is not None:
                w.append((sS, pr_free))
            dt_t = b.op("tensor", dtmm, w, (sT, 1))
            bt_v = b.op("vector", lambda v, tc=tc: v.tensor_copy(xsc[:, tc:tc + 512], pw),
                        [(sT, bt_t)], (sV, 1))
            pr_free = b.op("scalar", lambda s: s.activation(dts, pr, AF.Exp, bias=b_dt),
                           [(sT, dt_t), (sdw, w_small)], (sS, 1))
            sv2 = b.op("scalar", lambda s: s.activation(dts, dts, AF.Ln, bias=1.0),
                       [(sS, pr_free)], (sS, 1))
            vv = b.op("vector", lambda v: v.tensor_scalar_min(dts, dts, 10.0),
                      [(sS, sv2)], (sV, 1))
            dec_s = b.op("scalar", lambda s, tc=tc: s.activation(
                decay[:, tc:tc + 512], dts, AF.Exp, scale=nrate),
                [(sV, vv), (sV, nrate_v)], (sS, 1))
            b.op("vector", lambda v, tc=tc: v.tensor_mul(
                zt, decay[:, tc:tc + 512], xsc[:, tc:tc + 512]), [(sS, dec_s)])
            b.op("vector", lambda v, tc=tc: v.tensor_sub(
                xsc[:, tc:tc + 512], xsc[:, tc:tc + 512], zt), (), (sV, 1))

        prep_gate = prep2(W_in_d[DM:2 * DM], 8, 8, W_inT, 1024,
                          first_waits=[(sT, sT.n)])
        xl_done_til0 = build_xlnT_tile(0)
        scan1 = b.op("vector", lambda v: v.tensor_tensor_scan(
            hs, decay, xsc, 0.0, OP.mult, OP.add), (), (sV, 1))
        cs = gdma(cc_in_d.rearrange("o p -> p o"), hs[:, NT - 1:NT], [(sV, scan1)], sem=sccin)
        ccv = b.op("gpsimd", lambda g: g.collective_compute(
            "AllGather", OP.bypass, replica_groups=[list(range(8))],
            ins=[cc_in_d[:, :].opt()], outs=[cc_out_d[:, :].opt()]),
            [(sccin, cs)], (scc, 1))
        hv = gdma(hall, cc_out_d[:, :], [(scc, ccv)], sem=shall)
        seltv = b.op("tensor", lambda t: t.matmul(pr[:, 0:1], hall, hsel,
                                                  start=True, stop=True),
                     [(shall, hv), (sS, pr_free)], (sT, 1))
        hinv = b.op("vector", lambda v: v.tensor_copy(H_in, pr[:, 0:1]), [(sT, seltv)], (sV, 1))
        hs_done = b.op("vector", lambda v: v.tensor_tensor_scan(
            hs, decay, xsc, H_in, OP.mult, OP.add), [(sV, hinv)])

        sg_done = {}
        z_done = {}
        y_slot_free = {0: None, 1: None}
        zprev = {"n": None}
        g_slot_free = {0: None, 1: None}
        prev_out_ev = None
        hsbf_v = None
        pend_stats = None
        for til in range(NTIL):
            tc = til * 512
            xl_done = xl_done_til0 if til == 0 else build_xlnT_tile(til)
            if pend_stats is not None:
                ln_stats_raw(pend_stats)
                pend_stats = None
            w = [(sV, hs_done)]
            if hsbf_v is not None:
                w.append((sT, y_last))
            hsbf_v = b.op("vector", lambda v, tc=tc: v.tensor_copy(hs_bf, hs[:, tc:tc + 512]),
                          w, (sV, 1))
            y_last = None
            pend_z = None

            def emit_z(m, sg_v, y_t, yb, yoff, sgs):
                wz = [(sS, sg_v), (sT, y_t)]
                if zprev["n"] is not None:
                    wz.append((sG, zprev["n"]))
                ztv = b.op("vector", lambda v, yb=yb, yoff=yoff, sgs=sgs: v.tensor_mul(
                    zt, yb[:, yoff:yoff + 512], sgs), wz, (sV, 1))
                b.op("gpsimd", lambda g2, m=m: g2.tensor_scalar(
                    z_bf[:, m * 512:(m + 1) * 512], xlTt[:, m * 512:(m + 1) * 512],
                    D_sb[:, m:m + 1], None, OP.mult),
                    [(sV, ztv), (sdw, w_small)], (sG, 1))
                z_done[m] = b.op("gpsimd", lambda g2, m=m: g2.tensor_add(
                    z_bf[:, m * 512:(m + 1) * 512], z_bf[:, m * 512:(m + 1) * 512],
                    zt), (), (sG, 1))
                zprev["n"] = z_done[m]
                y_slot_free[m % 2] = z_done[m]

            for m in range(8):
                gb, goff = (pe0, 0) if m % 2 == 0 else (pe0, 512)

                def gmm(t, m=m, gb=gb, goff=goff):
                    r = None
                    for k in range(8):
                        r = t.matmul(gb[:, goff:goff + 512],
                                     W_inT[:, k * DM + m * 128:k * DM + (m + 1) * 128],
                                     xlTt[:, k * 512:(k + 1) * 512],
                                     start=(k == 0), stop=(k == 7))
                    return r
                w = [xl_done, (sS, prep_gate)]
                if g_slot_free[m % 2] is not None:
                    w.append((sS, g_slot_free[m % 2]))
                if til == 0 and m < 4 and prev_out_ev is None:
                    w.append((sS, slot_evac[m % 4]))
                if prev_out_ev is not None and m < 2:
                    w.append((sV, prev_out_ev))
                gate_t = b.op("tensor", gmm, w, (sT, 1))

                yb, yoff = (pe1, 0) if m % 2 == 0 else (pe1, 512)

                def ymm(t, m=m, yb=yb, yoff=yoff):
                    return t.matmul(yb[:, yoff:yoff + 512], W_cT[:, m * 128:(m + 1) * 128],
                                    hs_bf, start=True, stop=True)
                wy = [(sV, hsbf_v)]
                if y_slot_free[m % 2] is not None:
                    wy.append((sG, y_slot_free[m % 2]))
                y_t = b.op("tensor", ymm, wy, (sT, 1))
                y_last = y_t
                sgs = sg_b[:, (m % 2) * 512:(m % 2) * 512 + 512]
                wsl = [(sT, gate_t)]
                if (m - 2) in z_done:
                    wsl.append((sG, z_done[m - 2]))
                g_slot_free[m % 2] = b.op("scalar", lambda s, sgs=sgs, gb=gb, goff=goff:
                                          s.activation(sgs, gb[:, goff:goff + 512], AF.Silu),
                                          wsl, (sS, 1))
                sg_done[m] = g_slot_free[m % 2]
                if pend_z is not None:
                    emit_z(*pend_z)
                pend_z = (m, sg_done[m], y_t, yb, yoff, sgs)
            if pend_z is not None:
                emit_z(*pend_z)
                pend_z = None
            z_all = z_done[7]
            for gp in range(4):
                for hh in range(2):
                    ob, ooff = (pe0, 0) if hh == 0 else (pe0, 512)

                    def omm(t, gp=gp, hh=hh, ob=ob, ooff=ooff):
                        r = None
                        for k in range(8):
                            r = t.matmul(ob[:, ooff:ooff + 512],
                                         z_bf[:, k * 512 + gp * 128:k * 512 + (gp + 1) * 128],
                                         W_oT[:, k * DM + hh * 512:k * DM + hh * 512 + 512],
                                         start=(k == 0), stop=(k == 7))
                        return r
                    w = [(sG, z_all)]
                    if gp == 0:
                        w.append((sS, sg_done[6 + hh]))
                    if prev_out_ev is not None:
                        w.append((sV, prev_out_ev))
                    tv = b.op("tensor", omm, w, (sT, 1))
                    g = til * 4 + gp
                    rs = resid[:, g * DM + hh * 512:g * DM + hh * 512 + 512]
                    prev_out_ev = b.op("vector", lambda v, rs=rs, ob=ob, ooff=ooff:
                                       v.tensor_add(rs, ob[:, ooff:ooff + 512], rs),
                                       [(sT, tv)], (sV, 1))
            if stage >= 4 and til < 4:
                pend_stats = list(range(til * 4, til * 4 + 4))
        if pend_stats is not None:
            ln_stats_raw(pend_stats)
            pend_stats = None
        stats2 = ln_stats_fin(0, GH) if stage >= 4 else None
        return hs_done, stats2

    if DBG:
        dbg_d = {}
        for nm, shp in [("d_negmean", [128, G]), ("d_rstd", [128, G]),
                        ("d_xlnT", [128, DM]), ("d_logits", [128, GH * NE]),
                        ("d_wts", [128, GH * NE]), ("d_m1s", [128, GH]),
                        ("d_m2s", [128, GH]), ("d_svals", [128, GH]),
                        ("d_masks1", [128, GH * NE]), ("d_lg2", [128, GH * NE]),
                        ("d_acc", [128, DM]), ("d_Gb", [128, 2 * DM]),
                        ("d_combT", [128, 2 * DM]), ("d_woutsb", [128, 8 * DM])]:
            dbg_d[nm] = ap(nc.dram_tensor(nm, shp, F32, kind="ExternalOutput"))
    hs_done_v = None
    if stage >= 2:
        ffn(0)
        if DBG:
            dfence = fence()
            for nm, src in [("d_negmean", negmean), ("d_rstd", rstd),
                            ("d_logits", logits), ("d_wts", wts),
                            ("d_m1s", m1s), ("d_m2s", m2s), ("d_svals", svals),
                            ("d_masks1", masks1), ("d_lg2", lg2),
                            ("d_xlnT", x_lnT[:, 0:DM]), ("d_acc", acc[:, 0:DM]),
                            ("d_Gb", Gb[:, 0:2 * DM]), ("d_combT", combT[:, 0:2 * DM]),
                            ("d_woutsb", w_out[:, 0:8 * DM])]:
                gdma(dbg_d[nm][:, :], src, dfence, sem=sdo2[0])
    stats2 = None
    if stage >= 3:
        hs_done_v, stats2 = ssm()
    if stage >= 4:
        ffn(1, exp_early=[(sV, hs_done_v)] if hs_done_v else None, stats_in=stats2)
    if stage < 4:
        end_fence = fence()
        prev = None
        for g in range(G):
            w = [x_wait[g]] + (end_fence if g == 0 else [])
            if len(store_seq) >= 1:
                w.append(store_seq[-1])
            prev = b.op("vector", lambda v, g=g: v.tensor_copy(
                out_sb[:, 0:DM], resid[:, g * DM:(g + 1) * DM]), w, (sV, 1))
            store_seq.append((sdo2[0],
                              ydma(out_d[g * 128:(g + 1) * 128, :],
                                   out_sb[:, 0:DM], [(sV, prev)], sem=sdo2[0])))

    b.op("sync", lambda s: s.wait_ge(sdo2[0].h, sdo2[0].n))
    b.op("sync", lambda s: s.wait_ge(sdo2[1].h, sdo2[1].n))

    with nc.Block() as block:
        @block.gpsimd
        def _(g):
            b.emit("gpsimd", g)

        @block.tensor
        def _(t):
            b.emit("tensor", t)

        @block.scalar
        def _(s):
            b.emit("scalar", s)

        @block.vector
        def _(v):
            b.emit("vector", v)

        @block.sync
        def _(s):
            b.emit("sync", s)

    return nc


_CACHED = {}


def kernel(**inputs):
    import ml_dtypes
    x = np.asarray(inputs["L0_attn_x"])
    if STAGE not in _CACHED:
        _CACHED[STAGE] = build_program(STAGE)
    nc = _CACHED[STAGE]

    ident = np.eye(128, dtype=ml_dtypes.bfloat16)
    common = dict(
        W_in=np.ascontiguousarray(inputs["L0_attn_W_in"], np.float32),
        W_state=np.ascontiguousarray(inputs["L0_attn_W_state"], np.float32),
        W_dt=np.ascontiguousarray(inputs["L0_attn_W_dt"], np.float32),
        b_dt=np.ascontiguousarray(inputs["L0_attn_b_dt"], np.float32),
        A_log=np.ascontiguousarray(inputs["L0_attn_A_log"], np.float32),
        W_C=np.ascontiguousarray(inputs["L0_attn_W_C"], np.float32),
        Dvec=np.ascontiguousarray(inputs["L0_attn_D"], np.float32),
        W_out_ssm=np.ascontiguousarray(inputs["L0_attn_W_out"], np.float32),
        w_router=np.ascontiguousarray(inputs["L0_ffn_w_router"], np.float32),
        b_router=np.ascontiguousarray(inputs["L0_ffn_b_router"], np.float32),
        w_experts=np.ascontiguousarray(inputs["L0_ffn_w_experts"], np.float32),
        b_experts=np.ascontiguousarray(inputs["L0_ffn_b_experts"], np.float32),
        b_exp_bcast=np.ascontiguousarray(np.broadcast_to(
            np.asarray(inputs["L0_ffn_b_experts"], np.float32)[:, None, :], (8, 128, 1024))),
        w_out_ffn=np.ascontiguousarray(inputs["L0_ffn_w_out"], np.float32),
        b_out=np.ascontiguousarray(inputs["L0_ffn_b_out"], np.float32),
        ident=np.asarray(ident),
    )
    in_maps = []
    for c in range(8):
        bb, hh = c % 4, c // 4
        xs = np.ascontiguousarray(x[bb, hh * NT:(hh + 1) * NT, :], np.float32)
        sel = np.zeros((8, 1), np.float32)
        if hh == 1:
            sel[bb, 0] = 1.0
        m = dict(common)
        m["x"] = xs
        m["hsel"] = sel
        in_maps.append(m)
    res = run_bass_kernel_spmd(nc, in_maps, core_ids=list(range(8)))
    out = np.empty((4, 8192, 1024), np.float32)
    for c in range(8):
        bb, hh = c % 4, c // 4
        out[bb, hh * NT:(hh + 1) * NT, :] = res.results[c]["out"]
    return out


# version 15
# speedup vs baseline: 1.1238x; 1.1238x over previous
import numpy as np
import concourse.bass as bass
import concourse.mybir as mybir
from concourse.bass_utils import run_bass_kernel_spmd

F32 = mybir.dt.float32
BF16 = mybir.dt.bfloat16
AX = mybir.AxisListType.X
OP = mybir.AluOpType
AF = mybir.ActivationFunctionType

STAGE = 4
FFN_PARTS = 4
FFN_NEXP = 8
FFN_B2N = 4
FFN_NG = 16
DBG = 0

NT = 4096
G = 32
GH = 16
NTIL = 8
DM = 1024
DS = 128
NE = 8


class Sem:
    def __init__(self, nc, name):
        self.h = nc.alloc_semaphore(name)
        self.n = 0


class B:

    def __init__(self):
        self.ops = []
        self.auto = {}

    def op(self, eng, fn, waits=(), inc=None):
        w = [(s, v) for (s, v) in waits if v is not None and v > 0]
        if eng in self.auto:
            sem = self.auto[eng]
            if sem.n > 0:
                w.append((sem, sem.n))
            if inc is None:
                inc = (sem, 1)
        if inc is not None:
            s, d = inc
            s.n += d
        self.ops.append((eng, fn, w, inc))
        return inc[0].n if inc else None

    def emit(self, eng_name, eng):
        maxw = {}
        for (e, fn, waits, inc) in self.ops:
            if e != eng_name:
                continue
            for (s, v) in waits:
                k = id(s)
                if maxw.get(k, -1) >= v:
                    continue
                maxw[k] = v
                eng.wait_ge(s.h, v)
            inst = fn(eng)
            if inc is not None:
                inst.then_inc(inc[0].h, inc[1])


def build_program(stage=None):
    if stage is None:
        stage = STAGE
    nc = bass.Bass("TRN2", target_bir_lowering=False)
    ap = lambda t: t.ap()

    x_d = ap(nc.dram_tensor("x", [NT, DM], F32, kind="ExternalInput"))
    W_in_d = ap(nc.dram_tensor("W_in", [2 * DM, DM], F32, kind="ExternalInput"))
    W_state_d = ap(nc.dram_tensor("W_state", [DS, DM], F32, kind="ExternalInput"))
    W_dt_d = ap(nc.dram_tensor("W_dt", [DS, DM], F32, kind="ExternalInput"))
    b_dt_d = ap(nc.dram_tensor("b_dt", [DS], F32, kind="ExternalInput"))
    A_log_d = ap(nc.dram_tensor("A_log", [DS], F32, kind="ExternalInput"))
    W_C_d = ap(nc.dram_tensor("W_C", [DM, DS], F32, kind="ExternalInput"))
    D_d = ap(nc.dram_tensor("Dvec", [DM], F32, kind="ExternalInput"))
    W_outs_d = ap(nc.dram_tensor("W_out_ssm", [DM, DM], F32, kind="ExternalInput"))
    w_rt_d = ap(nc.dram_tensor("w_router", [DM, NE], F32, kind="ExternalInput"))
    b_rt_d = ap(nc.dram_tensor("b_router", [NE], F32, kind="ExternalInput"))
    w_exp_d = ap(nc.dram_tensor("w_experts", [NE, DM, DM], F32, kind="ExternalInput"))
    b_exp_d = ap(nc.dram_tensor("b_experts", [NE, DM], F32, kind="ExternalInput"))
    b_expb_d = ap(nc.dram_tensor("b_exp_bcast", [NE, 128, DM], F32, kind="ExternalInput"))
    w_out_d = ap(nc.dram_tensor("w_out_ffn", [DM, DM], F32, kind="ExternalInput"))
    b_out_d = ap(nc.dram_tensor("b_out", [DM], F32, kind="ExternalInput"))
    ident_d = ap(nc.dram_tensor("ident", [128, 128], BF16, kind="ExternalInput"))
    hsel_d = ap(nc.dram_tensor("hsel", [NE, 1], F32, kind="ExternalInput"))
    out_d = ap(nc.dram_tensor("out", [NT, DM], F32, kind="ExternalOutput"))
    cc_in_d = ap(nc.dram_tensor("cc_in", [1, DS], F32))
    cc_out_d = ap(nc.dram_tensor("cc_out", [NE, DS], F32))

    al = lambda *a: nc.alloc_sbuf_tensor(*a).ap()

    resid = al("resid", [128, G * DM], BF16)
    ident = al("identsb", [128, 128], BF16)
    ones1 = al("ones1", [1, 128], BF16)
    w_rt = al("w_rt", [128, 8 * NE], BF16)
    b_rt = al("b_rt", [1, NE], BF16)
    b_out = al("b_outsb", [1, DM], BF16)
    D_sb = al("D_sb", [128, 8], F32)
    b_dt = al("b_dtsb", [128, 1], F32)
    nrate = al("nrate", [128, 1], F32)
    hsel = al("hselsb", [NE, 1], F32)
    hall = al("hallsb", [NE, DS], F32)
    H_in = al("H_insb", [128, 1], F32)
    negmean = al("negmean", [128, G], F32)
    rstd = al("rstd", [128, G], F32)
    varn = al("varn", [128, G], F32)
    logits = al("logits", [128, GH * NE], F32)
    masks1 = al("masks1", [128, GH * NE], F32)
    lg2 = al("lg2", [128, GH * NE], F32)
    wts = al("wts", [128, GH * NE], F32)
    m1s = al("m1s", [128, GH], F32)
    m2s = al("m2s", [128, GH], F32)
    svals = al("svals", [128, GH], F32)
    scr = al("scr", [128, 8], F32)
    xlt = al("xlt", [128, 2 * DM], BF16)
    out_sb = al("out_sbuf", [128, DM], F32)
    b_bc = al("b_bc", [128, 2 * DM], BF16)

    pe0 = nc.alloc_psum_tensor("pe0", [128, 1024], F32).ap()
    pe1 = nc.alloc_psum_tensor("pe1", [128, 1024], F32).ap()
    pw = nc.alloc_psum_tensor("pw", [128, 512], F32).ap()
    pr = nc.alloc_psum_tensor("pr", [128, 512], F32).ap()
    ptA = nc.alloc_psum_tensor("ptA", [128, 1024], BF16).ap()
    ptB = nc.alloc_psum_tensor("ptB", [128, 1024], BF16).ap()

    from contextlib import ExitStack
    with ExitStack() as st_f:
        e = st_f.enter_context
        x_lnT = e(nc.sbuf_tensor("x_lnT", [128, GH * DM], BF16)).ap()
        Gb = e(nc.sbuf_tensor("Gb", [128, 2 * DM], BF16)).ap()
        combT = e(nc.sbuf_tensor("combT", [128, 2 * DM], BF16)).ap()
        expb = e(nc.sbuf_tensor("expb", [128, 2 * 8 * DM], BF16)).ap()
        acc = e(nc.sbuf_tensor("acc", [128, GH * DM], BF16)).ap()
        w_out = e(nc.sbuf_tensor("w_out", [128, 8 * DM], BF16)).ap()
    with ExitStack() as st_s:
        e = st_s.enter_context
        W_inT = e(nc.sbuf_tensor("W_inT", [128, 8 * DM], BF16)).ap()
        W_oT = e(nc.sbuf_tensor("W_oT", [128, 8 * DM], BF16)).ap()
        W_sT = e(nc.sbuf_tensor("W_sT", [128, 8 * DS], BF16)).ap()
        W_dT = e(nc.sbuf_tensor("W_dT", [128, 8 * DS], BF16)).ap()
        W_cT = e(nc.sbuf_tensor("W_cT", [128, DM], BF16)).ap()
        _pad = e(nc.sbuf_tensor("_pad", [128, DM], BF16)).ap()
        decay = e(nc.sbuf_tensor("decay", [128, NT], F32)).ap()
        xsc = e(nc.sbuf_tensor("xsc", [128, NT], F32)).ap()
        hs = e(nc.sbuf_tensor("hs", [128, NT], F32)).ap()
        hs_bf = e(nc.sbuf_tensor("hs_bf", [128, 512], BF16)).ap()
        xlTt = e(nc.sbuf_tensor("xlTt", [128, NT], BF16)).ap()
        u_bf = e(nc.sbuf_tensor("u_bf", [128, NT], BF16)).ap()
        z_bf = e(nc.sbuf_tensor("z_bf", [128, NT], BF16)).ap()
        sg_b = e(nc.sbuf_tensor("sg_b", [128, 2 * 512], BF16)).ap()
        dts = e(nc.sbuf_tensor("dts", [128, 512], F32)).ap()
        zt = e(nc.sbuf_tensor("zt", [128, 512], F32)).ap()
        stage_b = e(nc.sbuf_tensor("stage_b", [128, 2 * DM], BF16)).ap()

    b = B()
    sdw = Sem(nc, "sdw")
    sx = [Sem(nc, f"sx{q}") for q in range(8)]
    sexp = [Sem(nc, f"sexp{e}") for e in range(8)]
    sprep = Sem(nc, "sprep")
    sprep2 = Sem(nc, "sprep2")
    swout = Sem(nc, "swout")
    shall = Sem(nc, "shall")
    sccin = Sem(nc, "sccin")
    sdo2 = [Sem(nc, "sdo0"), Sem(nc, "sdo1")]
    sdo = Sem(nc, "sdo")
    scc = Sem(nc, "scc")
    sT = Sem(nc, "sT")
    sS = Sem(nc, "sS")
    sV = Sem(nc, "sV")
    sG = Sem(nc, "sG")
    sY = Sem(nc, "sY")
    ENG = {"tensor": sT, "scalar": sS, "vector": sV, "gpsimd": sG, "sync": sY}
    b.auto = {"vector": sV, "scalar": sS}

    def fence():
        return [(sT, sT.n), (sS, sS.n), (sV, sV.n)]

    def gdma(out, in_, waits=(), slow=False, sem=None):
        sem = sem if sem is not None else sdw
        return b.op("gpsimd", lambda g, o=out, i=in_, sl=slow: g.dma_start(
            out=o, in_=i, allow_slow_non_contiguous=sl), waits, (sem, 16))

    def ydma(out, in_, waits=(), sem=None):
        sem = sem if sem is not None else sdo
        return b.op("sync", lambda s, o=out, i=in_: s.dma_start(out=o, in_=i),
                    waits, (sem, 16))



    gdma(ident, ident_d[:, :])
    gdma(hsel, hsel_d[:, :])
    gdma(w_rt.rearrange("p (k e) -> p k e", e=NE),
         w_rt_d.rearrange("(k p) e -> p k e", p=128))
    gdma(b_rt, b_rt_d.unsqueeze(0))
    gdma(b_out, b_out_d.unsqueeze(0))
    gdma(D_sb, D_d.rearrange("(m p) -> p m", p=128), slow=True)
    gdma(b_dt, b_dt_d.unsqueeze(1))
    w_small = gdma(nrate, A_log_d.unsqueeze(1))
    X_SPANS = [(0, 1), (1, 1), (2, 1), (3, 1), (4, 2), (6, 2), (8, 4), (12, 4),
               (16, 16)]
    x_wait = {}
    for ci, (off, ngr) in enumerate(X_SPANS):
        sem = sx[ci] if ci < 8 else sdo
        v = gdma(resid[:, off * DM:(off + ngr) * DM].rearrange("p (g d) -> p g d", d=DM),
                 x_d[off * 128:(off + ngr) * 128, :].rearrange("(g p) d -> p g d", p=128),
                 sem=sem)
        for g in range(off, off + ngr):
            x_wait[g] = (sem, v)

    ones_v = b.op("vector", lambda v: v.memset(ones1, 1.0), (), (sV, 1))
    v0 = b.op("scalar", lambda s: s.activation(scr[:, 0:1], nrate, AF.Exp),
              [(sdw, w_small)], (sS, 1))
    v = b.op("scalar", lambda s: s.activation(scr[:, 0:1], scr[:, 0:1], AF.Ln, bias=1.0),
             [(sS, v0)], (sS, 1))
    nrate_v = b.op("vector", lambda v2: v2.tensor_scalar(nrate, scr[:, 0:1], 1e-4, -1.0,
                                                         OP.add, OP.mult), [(sS, v)], (sV, 1))

    xlt_guard = {0: None, 1: None}

    def ln_stats_raw(groups, first_waits=()):
        for i, g in enumerate(groups):
            rs = resid[:, g * DM:(g + 1) * DM]
            b.op("vector", lambda v, rs=rs, g=g: v.reduce_sum(
                negmean[:, g:g + 1], rs, AX, negate=True),
                [x_wait[g]] + (list(first_waits) if i == 0 else []))
        f = b.op("vector", lambda v, g0=groups[0], n=len(groups): v.tensor_scalar(
            negmean[:, g0:g0 + n], negmean[:, g0:g0 + n], 1.0 / DM, None, OP.mult),
            (), (sV, 1))
        for i, g in enumerate(groups):
            rs = resid[:, g * DM:(g + 1) * DM]
            b.op("vector", lambda v, rs=rs, g=g: v.scalar_tensor_tensor(
                out_sb[:, 0:DM], rs, negmean[:, g:g + 1], rs, OP.add, OP.mult,
                accum_out=varn[:, g:g + 1]), [(sV, f)] if i == 0 else ())

    def ln_stats_fin(g0, n):
        b.op("vector", lambda v: v.tensor_scalar(
            varn[:, g0:g0 + n], varn[:, g0:g0 + n], 1.0 / DM, 1e-5, OP.mult, OP.add))
        f2 = b.op("vector", lambda v: v.reciprocal(
            rstd[:, g0:g0 + n], varn[:, g0:g0 + n]), (), (sV, 1))
        f3 = b.op("scalar", lambda s: s.activation(
            rstd[:, g0:g0 + n], rstd[:, g0:g0 + n], AF.Sqrt), [(sV, f2)], (sS, 1))
        return (sS, f3)

    def ln_stats(groups, g0, first_waits=()):
        ln_stats_raw(groups, first_waits)
        return ln_stats_fin(g0, len(groups))

    def ln_stats_thunks(groups, g0):
        th = []
        box = {}
        for g in groups:
            th.append(lambda g=g: b.op("vector", lambda v, g=g: v.reduce_sum(
                negmean[:, g:g + 1], resid[:, g * DM:(g + 1) * DM], AX,
                negate=True), [x_wait[g]]))
        th.append(lambda g0=g0, n=len(groups): b.op(
            "vector", lambda v: v.tensor_scalar(
                negmean[:, g0:g0 + n], negmean[:, g0:g0 + n], 1.0 / DM, None,
                OP.mult)))
        for g in groups:
            th.append(lambda g=g: b.op("vector", lambda v, g=g: v.scalar_tensor_tensor(
                out_sb[:, 0:DM], resid[:, g * DM:(g + 1) * DM],
                negmean[:, g:g + 1], resid[:, g * DM:(g + 1) * DM],
                OP.add, OP.mult, accum_out=varn[:, g:g + 1])))

        def t_fin(g0=g0, n=len(groups)):
            box["h"] = ln_stats_fin(g0, n)
        th.append(t_fin)
        return th, box

    xln_prev = {"n": None}

    def x_ln_group(g, slot, stats_ready, resid_dep=None):
        rs = resid[:, g * DM:(g + 1) * DM]
        dst = xlt[:, slot * DM:(slot + 1) * DM]
        w = [stats_ready]
        if xlt_guard[slot] is not None:
            w.append((sT, xlt_guard[slot]))
        if xln_prev["n"] is not None:
            w.append((sG, xln_prev["n"]))
        if resid_dep is not None:
            w.append(resid_dep)
        r = b.op("gpsimd", lambda gg, rs=rs, dst=dst, g=g: gg.tensor_scalar(
            dst, rs, negmean[:, g:g + 1], rstd[:, g:g + 1], OP.add, OP.mult), w, (sG, 1))
        xln_prev["n"] = r
        return r

    pt_rot = {"i": 0, "free": [None, None], "banks": [ptA, ptB]}

    def transp8(src, xr_wait, slot=None, evac_eng="scalar"):
        i = pt_rot["i"] % 2
        pt_rot["i"] += 1
        pt = pt_rot["banks"][i]
        t_last = None
        for k in range(8):
            w = []
            if k == 0:
                w = [xr_wait]
                if pt_rot["free"][i] is not None:
                    w.append(pt_rot["free"][i])
                w = [x for x in w if x is not None]
            inc = (sT, 1) if k == 7 else None
            t_last = b.op("tensor", lambda t, k=k, src=src, pt=pt: t.transpose(
                pt[:, k * 128:(k + 1) * 128], src[:, k * 128:(k + 1) * 128], ident), w, inc)
        if slot is not None:
            xlt_guard[slot] = t_last
        return pt, i, t_last

    def transp8_evac(dst, xr_wait, src, slot=None, eng="scalar", dst3d=None):
        pt, i, tlast = transp8(src, xr_wait, slot=slot)
        d = dst if dst3d is None else dst3d
        if eng == "scalar":
            ev = b.op("scalar", lambda s2, d=d, pt=pt: s2.copy(
                d, pt[:, 0:1024] if dst3d is None else pt.rearrange("p (k c) -> p k c", c=128)),
                [(sT, tlast)], (sS, 1))
            sem = sS
        else:
            ev = b.op("vector", lambda v2, d=d, pt=pt: v2.tensor_copy(
                d, pt[:, 0:1024] if dst3d is None else pt.rearrange("p (k c) -> p k c", c=128)),
                [(sT, tlast)], (sV, 1))
            sem = sV
        pt_rot["free"][i] = (sem, ev)
        return (sem, ev), tlast

    store_seq = []

    def ffn(call, exp_early=None, stats_in=None):
        final = (call == 1 and stage >= 4)
        prefetched = {}
        if exp_early is not None:
            for e_ in (0, 1):
                slot = e_ % 2
                dst = expb[:, slot * 8 * DM:(slot + 1) * 8 * DM].rearrange(
                    "p (k d) -> p k d", d=DM)
                src = w_exp_d[e_].rearrange("(k p) d -> p k d", p=128)
                gdma(dst, src, list(exp_early) + [(sdw, w_small)], sem=sexp[e_])
                prefetched[e_] = gdma(b_bc[:, slot * DM:(slot + 1) * DM],
                                      b_expb_d[e_], sem=sexp[e_])
        wo_v = gdma(w_out.rearrange("p (k e) -> p k e", e=DM),
                    w_out_d.rearrange("(k p) e -> p k e", p=128), fence(), sem=swout)
        call_fence = fence()
        if stats_in is None:
            smap = {}
            pend_chunks = [sp for sp in X_SPANS if sp[0] < GH]
            bg_thunks = []
            bg_box = {}
            for (coff, cn) in [sp for sp in X_SPANS if sp[0] >= GH]:
                th, box = ln_stats_thunks(list(range(coff, coff + cn)), coff)
                bg_thunks += th
                for gg in range(coff, coff + cn):
                    bg_box[gg] = box
        else:
            smap = {g: stats_in for g in range(GH)}
            pend_chunks = []
            bg_thunks = []
            bg_box = {}
            th, box = ln_stats_thunks(list(range(GH, G)), GH)
            bg_thunks += th
            for gg in range(GH, G):
                bg_box[gg] = box
        prpw_free = {0: None, 1: None}
        exp_last_mm = {}
        load_vals = {}
        badd_last = {}
        exp_use = {}
        for half in range(2):
            half_fence = fence()
            if half == 1:
                while bg_thunks:
                    bg_thunks.pop(0)()
                for gg, box in bg_box.items():
                    smap[gg] = box["h"]
            groups = list(range(half * GH, (half + 1) * GH))[:FFN_NG]
            ptA_free = None
            prev_rt_evac = None
            xlnT_ev = {}
            pend_b2 = None

            def b2_tail(pt, i, tlast, gl, g):
                dst = x_lnT[:, gl * DM:(gl + 1) * DM]
                ev = b.op("vector", lambda v, dst=dst, pt=pt: v.tensor_copy(
                    dst, pt[:, 0:1024]), [(sT, tlast)], (sV, 1))
                pt_rot["free"][i] = (sV, ev)
                xlnT_ev[g] = (sV, ev)

                def rt(t, gl=gl):
                    for k in range(8):
                        t.matmul(pr[:, 0:NE], x_lnT[:, (gl * 8 + k) * 128:(gl * 8 + k + 1) * 128],
                                 w_rt[:, k * NE:(k + 1) * NE], start=(k == 0), stop=False)
                    return t.matmul(pr[:, 0:NE], ones1, b_rt, start=False, stop=True)
                w = [xlnT_ev[g], (sdw, w_small), (sV, ones_v)]
                if b2_state["prev_rt"] is not None:
                    w.append((sS, b2_state["prev_rt"]))
                if prpw_free[1] is not None:
                    w.append(prpw_free[1])
                    prpw_free[1] = None
                tv = b.op("tensor", rt, w, (sT, 1))
                b2_state["prev_rt"] = b.op("scalar", lambda s, gl=gl: s.copy(
                    logits[:, gl * NE:(gl + 1) * NE], pr[:, 0:NE]), [(sT, tv)], (sS, 1))

            b2_state = {"prev_rt": prev_rt_evac}
            for j, g in enumerate(groups):
                gl = g - half * GH
                slot = j % 2
                while pend_chunks and pend_chunks[0][0] <= g:
                    coff, cn = pend_chunks.pop(0)
                    h = ln_stats(list(range(coff, coff + cn)), coff,
                                 call_fence if coff == 0 else ())
                    for gg in range(coff, coff + cn):
                        smap[gg] = h
                xr = x_ln_group(g, slot, smap[g])
                pt, i, tlast = transp8(xlt[:, slot * DM:(slot + 1) * DM], (sG, xr), slot=slot)
                if pend_b2 is not None:
                    b2_tail(*pend_b2)
                pend_b2 = (pt, i, tlast, gl, g)
            if pend_b2 is not None:
                b2_tail(*pend_b2)
                pend_b2 = None
            prev_rt_evac = b2_state["prev_rt"]
            if FFN_PARTS < 2:
                continue
            lg3 = logits.rearrange("p (g e) -> p g e", e=NE)
            m1f = b.op("vector", lambda v: v.tensor_reduce(m1s[:, 0:GH], lg3, AX, OP.max),
                       [(sS, prev_rt_evac)], (sV, 1))
            for j in range(GH):
                b.op("vector", lambda v, j=j: v.tensor_single_scalar(
                    masks1[:, j * NE:(j + 1) * NE], logits[:, j * NE:(j + 1) * NE],
                    m1s[:, j:j + 1], OP.is_ge), [(sV, m1f)] if j == 0 else ())
                b.op("vector", lambda v, j=j: v.scalar_tensor_tensor(
                    lg2[:, j * NE:(j + 1) * NE], masks1[:, j * NE:(j + 1) * NE], -1e30,
                    logits[:, j * NE:(j + 1) * NE], OP.mult, OP.add))
            lg23 = lg2.rearrange("p (g e) -> p g e", e=NE)
            b.op("vector", lambda v: v.tensor_reduce(m2s[:, 0:GH], lg23, AX, OP.max))
            dv = b.op("vector", lambda v: v.tensor_sub(svals[:, 0:GH], m1s[:, 0:GH],
                                                       m2s[:, 0:GH]), (), (sV, 1))
            sf = b.op("scalar", lambda s: s.activation(svals[:, 0:GH], svals[:, 0:GH],
                                                       AF.Sigmoid), [(sV, dv)], (sS, 1))
            for j in range(GH):
                b.op("vector", lambda v, j=j: v.tensor_single_scalar(
                    wts[:, j * NE:(j + 1) * NE], lg2[:, j * NE:(j + 1) * NE],
                    m2s[:, j:j + 1], OP.is_ge), [(sS, sf)] if j == 0 else ())
                b.op("vector", lambda v, j=j: v.scalar_tensor_tensor(
                    wts[:, j * NE:(j + 1) * NE], wts[:, j * NE:(j + 1) * NE],
                    svals[:, j:j + 1], wts[:, j * NE:(j + 1) * NE], OP.mult, OP.subtract))
                b.op("vector", lambda v, j=j: v.scalar_tensor_tensor(
                    wts[:, j * NE:(j + 1) * NE], masks1[:, j * NE:(j + 1) * NE],
                    svals[:, j:j + 1], wts[:, j * NE:(j + 1) * NE], OP.mult, OP.subtract))
            wtsf = b.op("vector", lambda v: v.tensor_copy(scr[:, 0:1], wts[:, 0:1]),
                        (), (sV, 1))
            if FFN_PARTS < 3:
                continue

            gelu_done = {}
            stt_done = {}
            badd_done = {}
            pend_stt = None

            def emit_stt(e_, g, gl, gslot, sv_):
                accg = acc[:, gl * DM:(gl + 1) * DM]
                if e_ == 0:
                    vv = b.op("vector", lambda v, accg=accg, gslot=gslot, gl=gl:
                              v.tensor_single_scalar(
                                  accg, Gb[:, gslot * DM:(gslot + 1) * DM],
                                  wts[:, gl * NE:gl * NE + 1], OP.mult),
                              [(sS, sv_), (sV, wtsf)], (sV, 1))
                else:
                    vv = b.op("vector", lambda v, accg=accg, gslot=gslot, gl=gl, e_=e_:
                              v.scalar_tensor_tensor(
                                  accg, Gb[:, gslot * DM:(gslot + 1) * DM],
                                  wts[:, gl * NE + e_:gl * NE + e_ + 1],
                                  accg, OP.mult, OP.add), [(sS, sv_)], (sV, 1))
                stt_done[(e_, g)] = vv
            for e_ in range(FFN_NEXP):
                slot = e_ % 2
                if half == 0 and e_ in prefetched:
                    load_vals[e_] = prefetched[e_]
                else:
                    w = [(sdw, w_small)]
                    if half == 0 and e_ < 2:
                        w += list(half_fence)
                    if e_ >= 2 or half == 1:
                        prev_e = e_ - 2 if e_ >= 2 else e_ + 6
                        w.append((sT, exp_last_mm[prev_e]))
                        w.append((sV, badd_last[prev_e]))
                    dst = expb[:, slot * 8 * DM:(slot + 1) * 8 * DM].rearrange(
                        "p (k d) -> p k d", d=DM)
                    src = w_exp_d[e_].rearrange("(k p) d -> p k d", p=128)
                    gdma(dst, src, w, sem=sexp[e_])
                    load_vals[e_] = gdma(
                        b_bc[:, slot * DM:(slot + 1) * DM], b_expb_d[e_], sem=sexp[e_])
                for j, g in enumerate(groups):
                    gl = g - half * GH
                    bank = pe0 if (j % 2 == 0) else pe1

                    def emm(t, gl=gl, bank=bank, slot=slot):
                        r = None
                        for k in range(8):
                            lhs = x_lnT[:, (gl * 8 + k) * 128:(gl * 8 + k + 1) * 128]
                            rhs0 = expb[:, slot * 8 * DM + k * DM:slot * 8 * DM + k * DM + 512]
                            rhs1 = expb[:, slot * 8 * DM + k * DM + 512:slot * 8 * DM + (k + 1) * DM]
                            t.matmul(bank[:, 0:512], lhs, rhs0, start=(k == 0), stop=(k == 7))
                            r = t.matmul(bank[:, 512:1024], lhs, rhs1, start=(k == 0), stop=(k == 7))
                        return r
                    w = [(sexp[e_], load_vals[e_]), xlnT_ev[g]]
                    if j >= 2:
                        w.append((sV, badd_done[(e_, groups[j - 2])]))
                    elif e_ > 0:
                        w.append((sV, badd_done[(e_ - 1, groups[GH - 2 + j])]))
                    tv = b.op("tensor", emm, w, (sT, 1))
                    exp_last_mm[e_] = tv
                    gslot = j % 2
                    wV = [(sT, tv)]
                    if j >= 2:
                        wV.append((sS, gelu_done[(e_, groups[j - 2])]))
                    elif e_ > 0:
                        wV.append((sS, gelu_done[(e_ - 1, groups[GH - 2 + j])]))
                    bv = b.op("vector", lambda v, bank=bank, gslot=gslot, slot=slot:
                              v.tensor_add(Gb[:, gslot * DM:(gslot + 1) * DM],
                                           bank[:, 0:1024],
                                           b_bc[:, slot * DM:(slot + 1) * DM]),
                              wV, (sV, 1))
                    badd_done[(e_, g)] = bv
                    badd_last[e_] = bv
                    wS = [(sV, bv)]
                    if j >= 2:
                        wS.append((sV, stt_done[(e_, groups[j - 2])]))
                    elif e_ > 0:
                        wS.append((sV, stt_done[(e_ - 1, groups[GH - 2 + j])]))
                    sv_ = b.op("scalar", lambda s, gslot=gslot: s.activation(
                        Gb[:, gslot * DM:(gslot + 1) * DM],
                        Gb[:, gslot * DM:(gslot + 1) * DM],
                        AF.Gelu_apprx_tanh), wS, (sS, 1))
                    gelu_done[(e_, g)] = sv_
                    if pend_stt is not None:
                        emit_stt(*pend_stt)
                    pend_stt = (e_, g, gl, gslot, sv_)
                    if e_ >= 4 and bg_thunks:
                        bg_thunks.pop(0)()

            if pend_stt is not None:
                emit_stt(*pend_stt)
                pend_stt = None
            if FFN_PARTS < 4:
                continue
            pend_b5 = None
            b5_state = {"wout_last": None}

            def b5_tail(ev, slot, j, g):
                pv = None
                for hh in range(2):
                    wb = (pw, 0) if (j * 2 + hh) % 2 == 0 else (pr, 1)
                    wbank, wbi = wb

                    def wmm(t, slot=slot, hh=hh, wbank=wbank):
                        t.matmul(wbank, ones1, b_out[:, hh * 512:(hh + 1) * 512],
                                 start=True, stop=False)
                        r = None
                        for k in range(8):
                            lhs = combT[:, slot * DM + k * 128:slot * DM + (k + 1) * 128]
                            rhs = w_out[:, k * DM + hh * 512:k * DM + hh * 512 + 512]
                            r = t.matmul(wbank, lhs, rhs, start=False, stop=(k == 7))
                        return r
                    w = [(sS, ev), (swout, wo_v), (sV, ones_v)]
                    if prpw_free[wbi] is not None:
                        w.append(prpw_free[wbi])
                    tv = b.op("tensor", wmm, w, (sT, 1))
                    b5_state["wout_last"] = tv
                    rs = resid[:, g * DM + hh * 512:g * DM + hh * 512 + 512]
                    if final:
                        dst = out_sb[:, hh * 512:hh * 512 + 512]
                        wV = [(sT, tv)]
                        if hh == 0 and len(store_seq) >= 1:
                            wV.append(store_seq[-1])
                        pv = b.op("vector", lambda v, dst=dst, rs=rs, wbank=wbank:
                                  v.scalar_tensor_tensor(dst, wbank, 0.5, rs,
                                                         OP.mult, OP.add), wV, (sV, 1))
                    else:
                        pv = b.op("vector", lambda v, rs=rs, wbank=wbank:
                                  v.scalar_tensor_tensor(rs, wbank, 0.5, rs,
                                                         OP.mult, OP.add), [(sT, tv)], (sV, 1))
                    prpw_free[wbi] = (sV, pv)
                if final:
                    sv2 = ydma(out_d[g * 128:(g + 1) * 128, :],
                               out_sb[:, 0:DM], [(sV, pv)], sem=sdo2[0])
                    store_seq.append((sdo2[0], sv2))

            for j, g in enumerate(groups):
                gl = g - half * GH
                slot = j % 2
                accg = acc[:, gl * DM:(gl + 1) * DM]
                ctd = combT[:, slot * DM:(slot + 1) * DM]
                pt, i, tlast = transp8(accg, (sV, stt_done[(FFN_NEXP - 1, g)]))
                wE = [(sT, tlast)]
                if b5_state["wout_last"] is not None and j >= 2:
                    wE.append((sT, b5_state["wout_last"]))
                ev = b.op("scalar", lambda s2, ctd=ctd, pt=pt: s2.copy(ctd, pt[:, 0:1024]),
                          wE, (sS, 1))
                pt_rot["free"][i] = (sS, ev)
                if pend_b5 is not None:
                    b5_tail(*pend_b5)
                pend_b5 = (ev, slot, j, g)
            if pend_b5 is not None:
                b5_tail(*pend_b5)
                pend_b5 = None

    def ssm():
        ssm_fence = fence()
        prev_t = {0: None, 1: None}

        def prep2(src_rows, n_rchunks, n_k, dstT, dst_cols_per_k, first_waits=()):
            last_t = None
            for r in range(n_rchunks):
                w = list(first_waits) if r == 0 else []
                if prev_t[r % 2] is not None:
                    w.append((sT, prev_t[r % 2]))
                ps_ = sprep if r % 2 == 0 else sprep2
                w.append((ps_, ps_.n))
                ld = gdma(stage_b[:, r % 2 * 1024:r % 2 * 1024 + n_k * 128],
                          src_rows[r * 128:(r + 1) * 128, :], w, sem=ps_)
                i = pt_rot["i"] % 2
                pt_rot["i"] += 1
                pt = pt_rot["banks"][i]
                for k in range(n_k):
                    ww = []
                    if k == 0:
                        ww = [(ps_, ld)]
                        if pt_rot["free"][i] is not None:
                            ww.append(pt_rot["free"][i])
                    inc = (sT, 1) if k == n_k - 1 else None
                    sb_off = (r % 2) * 1024
                    last_t = b.op("tensor", lambda t, k=k, pt=pt, sb_off=sb_off: t.transpose(
                        pt[:, k * 128:(k + 1) * 128],
                        stage_b[:, sb_off + k * 128:sb_off + (k + 1) * 128],
                        ident), ww, inc)
                dv2 = dstT.rearrange("p (k c) -> p k c", c=dst_cols_per_k)[
                    :, 0:n_k, r * 128:(r + 1) * 128]
                sv2 = pt.rearrange("p (k c) -> p k c", c=128)[:, 0:n_k, :]
                ev = b.op("scalar", lambda s, dv2=dv2, sv2=sv2: s.copy(dv2, sv2),
                          [(sT, last_t)], (sS, 1))
                pt_rot["free"][i] = (sS, ev)
                prev_t[r % 2] = last_t
            return ev

        prep2(W_in_d[0:DM], 8, 8, W_inT, 1024, ssm_fence)
        prep2(W_outs_d, 8, 8, W_oT, 1024)
        prep2(W_state_d, 1, 8, W_sT, 128)
        prep2(W_dt_d, 1, 8, W_dT, 128)
        prep_last = prep2(W_C_d, 8, 1, W_cT, 1024)

        smap3 = {}
        for c3 in range(8):
            h3 = ln_stats(list(range(c3 * 4, c3 * 4 + 4)), c3 * 4,
                          ssm_fence if c3 == 0 else ())
            for gg in range(c3 * 4, c3 * 4 + 4):
                smap3[gg] = h3

        def build_xlnT_tile(til, resid_dep=None):
            ptf = None
            for gp in range(4):
                g = til * 4 + gp
                slot = gp % 2
                xr = x_ln_group(g, slot, smap3[g], resid_dep=resid_dep)
                dv2 = xlTt.rearrange("p (k c) -> p k c", c=512)[:, :, gp * 128:(gp + 1) * 128]
                (es, ev), _tl = transp8_evac(None, (sG, xr),
                                             xlt[:, slot * DM:(slot + 1) * DM],
                                             slot=slot, eng="scalar", dst3d=dv2)
                ptf = (es, ev)
            return ptf

        slot_evac = {0: None, 1: None, 2: None, 3: None}
        pw_free = None
        pr_free = None
        dec_s = None
        bt_v = None
        for til in range(NTIL):
            xl_done = build_xlnT_tile(til)
            tc = til * 512
            uslots = [(pe0, 0), (pe0, 512), (pe1, 0), (pe1, 512)]
            u_ready = None
            for m in range(8):
                bank, off = uslots[m % 4]

                def umm(t, m=m, bank=bank, off=off):
                    r = None
                    for k in range(8):
                        r = t.matmul(bank[:, off:off + 512],
                                     W_inT[:, k * DM + m * 128:k * DM + (m + 1) * 128],
                                     xlTt[:, k * 512:(k + 1) * 512],
                                     start=(k == 0), stop=(k == 7))
                    return r
                w = [xl_done]
                if slot_evac[m % 4] is not None:
                    w.append((sS, slot_evac[m % 4]))
                tv = b.op("tensor", umm, w, (sT, 1))
                ev = b.op("scalar", lambda s, m=m, bank=bank, off=off: s.copy(
                    u_bf[:, m * 512:(m + 1) * 512], bank[:, off:off + 512]),
                    [(sT, tv)], (sS, 1))
                slot_evac[m % 4] = ev
                u_ready = ev

            def btmm(t):
                r = None
                for k in range(8):
                    r = t.matmul(pw, W_sT[:, k * 128:(k + 1) * 128],
                                 u_bf[:, k * 512:(k + 1) * 512], start=(k == 0), stop=(k == 7))
                return r
            w = [(sS, u_ready)]
            if bt_v is not None:
                w.append((sV, bt_v))
            bt_t = b.op("tensor", btmm, w, (sT, 1))

            def dtmm(t):
                r = None
                for k in range(8):
                    r = t.matmul(pr, W_dT[:, k * 128:(k + 1) * 128],
                                 u_bf[:, k * 512:(k + 1) * 512], start=(k == 0), stop=(k == 7))
                return r
            w = [(sS, u_ready)]
            if pr_free is not None:
                w.append((sS, pr_free))
            dt_t = b.op("tensor", dtmm, w, (sT, 1))
            bt_v = b.op("vector", lambda v, tc=tc: v.tensor_copy(xsc[:, tc:tc + 512], pw),
                        [(sT, bt_t)], (sV, 1))
            pr_free = b.op("scalar", lambda s: s.activation(dts, pr, AF.Exp, bias=b_dt),
                           [(sT, dt_t), (sdw, w_small)], (sS, 1))
            sv2 = b.op("scalar", lambda s: s.activation(dts, dts, AF.Ln, bias=1.0),
                       [(sS, pr_free)], (sS, 1))
            vv = b.op("vector", lambda v: v.tensor_scalar_min(dts, dts, 10.0),
                      [(sS, sv2)], (sV, 1))
            dec_s = b.op("scalar", lambda s, tc=tc: s.activation(
                decay[:, tc:tc + 512], dts, AF.Exp, scale=nrate),
                [(sV, vv), (sV, nrate_v)], (sS, 1))
            b.op("vector", lambda v, tc=tc: v.tensor_mul(
                zt, decay[:, tc:tc + 512], xsc[:, tc:tc + 512]), [(sS, dec_s)])
            b.op("vector", lambda v, tc=tc: v.tensor_sub(
                xsc[:, tc:tc + 512], xsc[:, tc:tc + 512], zt), (), (sV, 1))

        prep_gate = prep2(W_in_d[DM:2 * DM], 8, 8, W_inT, 1024,
                          first_waits=[(sT, sT.n)])
        xl_done_til0 = build_xlnT_tile(0)
        scan1 = b.op("vector", lambda v: v.tensor_tensor_scan(
            hs, decay, xsc, 0.0, OP.mult, OP.add), (), (sV, 1))
        cs = gdma(cc_in_d.rearrange("o p -> p o"), hs[:, NT - 1:NT], [(sV, scan1)], sem=sccin)
        ccv = b.op("gpsimd", lambda g: g.collective_compute(
            "AllGather", OP.bypass, replica_groups=[list(range(8))],
            ins=[cc_in_d[:, :].opt()], outs=[cc_out_d[:, :].opt()]),
            [(sccin, cs)], (scc, 1))
        hv = gdma(hall, cc_out_d[:, :], [(scc, ccv)], sem=shall)
        seltv = b.op("tensor", lambda t: t.matmul(pr[:, 0:1], hall, hsel,
                                                  start=True, stop=True),
                     [(shall, hv), (sS, pr_free)], (sT, 1))
        hinv = b.op("vector", lambda v: v.tensor_copy(H_in, pr[:, 0:1]), [(sT, seltv)], (sV, 1))
        hs_done = b.op("vector", lambda v: v.tensor_tensor_scan(
            hs, decay, xsc, H_in, OP.mult, OP.add), [(sV, hinv)])

        sg_done = {}
        z_done = {}
        y_slot_free = {0: None, 1: None}
        g_slot_free = {0: None, 1: None}
        prev_out_ev = None
        hsbf_v = None
        pend_stats = None
        for til in range(NTIL):
            tc = til * 512
            xl_done = xl_done_til0 if til == 0 else build_xlnT_tile(til)
            if pend_stats is not None:
                ln_stats_raw(pend_stats)
                pend_stats = None
            w = [(sV, hs_done)]
            if hsbf_v is not None:
                w.append((sT, y_last))
            hsbf_v = b.op("vector", lambda v, tc=tc: v.tensor_copy(hs_bf, hs[:, tc:tc + 512]),
                          w, (sV, 1))
            y_last = None
            pend_z = None

            def emit_z(m, sg_v, y_t, yb, yoff, sgs):
                b.op("vector", lambda v, yb=yb, yoff=yoff, sgs=sgs: v.tensor_mul(
                    zt, yb[:, yoff:yoff + 512], sgs),
                    [(sS, sg_v), (sT, y_t)])
                z_done[m] = b.op("vector", lambda v, m=m: v.scalar_tensor_tensor(
                    z_bf[:, m * 512:(m + 1) * 512], xlTt[:, m * 512:(m + 1) * 512],
                    D_sb[:, m:m + 1], zt, OP.mult, OP.add), (), (sV, 1))
                y_slot_free[m % 2] = z_done[m]

            for m in range(8):
                gb, goff = (pe0, 0) if m % 2 == 0 else (pe0, 512)

                def gmm(t, m=m, gb=gb, goff=goff):
                    r = None
                    for k in range(8):
                        r = t.matmul(gb[:, goff:goff + 512],
                                     W_inT[:, k * DM + m * 128:k * DM + (m + 1) * 128],
                                     xlTt[:, k * 512:(k + 1) * 512],
                                     start=(k == 0), stop=(k == 7))
                    return r
                w = [xl_done, (sS, prep_gate)]
                if g_slot_free[m % 2] is not None:
                    w.append((sS, g_slot_free[m % 2]))
                if til == 0 and m < 4 and prev_out_ev is None:
                    w.append((sS, slot_evac[m % 4]))
                if prev_out_ev is not None and m < 2:
                    w.append((sV, prev_out_ev))
                gate_t = b.op("tensor", gmm, w, (sT, 1))

                yb, yoff = (pe1, 0) if m % 2 == 0 else (pe1, 512)

                def ymm(t, m=m, yb=yb, yoff=yoff):
                    return t.matmul(yb[:, yoff:yoff + 512], W_cT[:, m * 128:(m + 1) * 128],
                                    hs_bf, start=True, stop=True)
                wy = [(sV, hsbf_v)]
                if y_slot_free[m % 2] is not None:
                    wy.append((sV, y_slot_free[m % 2]))
                y_t = b.op("tensor", ymm, wy, (sT, 1))
                y_last = y_t
                sgs = sg_b[:, (m % 2) * 512:(m % 2) * 512 + 512]
                wsl = [(sT, gate_t)]
                if (m - 2) in z_done:
                    wsl.append((sV, z_done[m - 2]))
                g_slot_free[m % 2] = b.op("scalar", lambda s, sgs=sgs, gb=gb, goff=goff:
                                          s.activation(sgs, gb[:, goff:goff + 512], AF.Silu),
                                          wsl, (sS, 1))
                sg_done[m] = g_slot_free[m % 2]
                if pend_z is not None:
                    emit_z(*pend_z)
                pend_z = (m, sg_done[m], y_t, yb, yoff, sgs)
            if pend_z is not None:
                emit_z(*pend_z)
                pend_z = None
            z_all = z_done[7]
            for gp in range(4):
                for hh in range(2):
                    ob, ooff = (pe0, 0) if hh == 0 else (pe0, 512)

                    def omm(t, gp=gp, hh=hh, ob=ob, ooff=ooff):
                        r = None
                        for k in range(8):
                            r = t.matmul(ob[:, ooff:ooff + 512],
                                         z_bf[:, k * 512 + gp * 128:k * 512 + (gp + 1) * 128],
                                         W_oT[:, k * DM + hh * 512:k * DM + hh * 512 + 512],
                                         start=(k == 0), stop=(k == 7))
                        return r
                    w = [(sV, z_all)]
                    if gp == 0:
                        w.append((sS, sg_done[6 + hh]))
                    if prev_out_ev is not None:
                        w.append((sV, prev_out_ev))
                    tv = b.op("tensor", omm, w, (sT, 1))
                    g = til * 4 + gp
                    rs = resid[:, g * DM + hh * 512:g * DM + hh * 512 + 512]
                    prev_out_ev = b.op("vector", lambda v, rs=rs, ob=ob, ooff=ooff:
                                       v.tensor_add(rs, ob[:, ooff:ooff + 512], rs),
                                       [(sT, tv)], (sV, 1))
            if stage >= 4 and til < 4:
                pend_stats = list(range(til * 4, til * 4 + 4))
        if pend_stats is not None:
            ln_stats_raw(pend_stats)
            pend_stats = None
        stats2 = ln_stats_fin(0, GH) if stage >= 4 else None
        return hs_done, stats2

    if DBG:
        dbg_d = {}
        for nm, shp in [("d_negmean", [128, G]), ("d_rstd", [128, G]),
                        ("d_xlnT", [128, DM]), ("d_logits", [128, GH * NE]),
                        ("d_wts", [128, GH * NE]), ("d_m1s", [128, GH]),
                        ("d_m2s", [128, GH]), ("d_svals", [128, GH]),
                        ("d_masks1", [128, GH * NE]), ("d_lg2", [128, GH * NE]),
                        ("d_acc", [128, DM]), ("d_Gb", [128, 2 * DM]),
                        ("d_combT", [128, 2 * DM]), ("d_woutsb", [128, 8 * DM])]:
            dbg_d[nm] = ap(nc.dram_tensor(nm, shp, F32, kind="ExternalOutput"))
    hs_done_v = None
    if stage >= 2:
        ffn(0)
        if DBG:
            dfence = fence()
            for nm, src in [("d_negmean", negmean), ("d_rstd", rstd),
                            ("d_logits", logits), ("d_wts", wts),
                            ("d_m1s", m1s), ("d_m2s", m2s), ("d_svals", svals),
                            ("d_masks1", masks1), ("d_lg2", lg2),
                            ("d_xlnT", x_lnT[:, 0:DM]), ("d_acc", acc[:, 0:DM]),
                            ("d_Gb", Gb[:, 0:2 * DM]), ("d_combT", combT[:, 0:2 * DM]),
                            ("d_woutsb", w_out[:, 0:8 * DM])]:
                gdma(dbg_d[nm][:, :], src, dfence, sem=sdo2[0])
    stats2 = None
    if stage >= 3:
        hs_done_v, stats2 = ssm()
    if stage >= 4:
        ffn(1, exp_early=[(sV, hs_done_v)] if hs_done_v else None, stats_in=stats2)
    if stage < 4:
        end_fence = fence()
        prev = None
        for g in range(G):
            w = [x_wait[g]] + (end_fence if g == 0 else [])
            if len(store_seq) >= 1:
                w.append(store_seq[-1])
            prev = b.op("vector", lambda v, g=g: v.tensor_copy(
                out_sb[:, 0:DM], resid[:, g * DM:(g + 1) * DM]), w, (sV, 1))
            store_seq.append((sdo2[0],
                              ydma(out_d[g * 128:(g + 1) * 128, :],
                                   out_sb[:, 0:DM], [(sV, prev)], sem=sdo2[0])))

    b.op("sync", lambda s: s.wait_ge(sdo2[0].h, sdo2[0].n))
    b.op("sync", lambda s: s.wait_ge(sdo2[1].h, sdo2[1].n))

    with nc.Block() as block:
        @block.gpsimd
        def _(g):
            b.emit("gpsimd", g)

        @block.tensor
        def _(t):
            b.emit("tensor", t)

        @block.scalar
        def _(s):
            b.emit("scalar", s)

        @block.vector
        def _(v):
            b.emit("vector", v)

        @block.sync
        def _(s):
            b.emit("sync", s)

    return nc


_CACHED = {}


def kernel(**inputs):
    import ml_dtypes
    x = np.asarray(inputs["L0_attn_x"])
    if STAGE not in _CACHED:
        _CACHED[STAGE] = build_program(STAGE)
    nc = _CACHED[STAGE]

    ident = np.eye(128, dtype=ml_dtypes.bfloat16)
    common = dict(
        W_in=np.ascontiguousarray(inputs["L0_attn_W_in"], np.float32),
        W_state=np.ascontiguousarray(inputs["L0_attn_W_state"], np.float32),
        W_dt=np.ascontiguousarray(inputs["L0_attn_W_dt"], np.float32),
        b_dt=np.ascontiguousarray(inputs["L0_attn_b_dt"], np.float32),
        A_log=np.ascontiguousarray(inputs["L0_attn_A_log"], np.float32),
        W_C=np.ascontiguousarray(inputs["L0_attn_W_C"], np.float32),
        Dvec=np.ascontiguousarray(inputs["L0_attn_D"], np.float32),
        W_out_ssm=np.ascontiguousarray(inputs["L0_attn_W_out"], np.float32),
        w_router=np.ascontiguousarray(inputs["L0_ffn_w_router"], np.float32),
        b_router=np.ascontiguousarray(inputs["L0_ffn_b_router"], np.float32),
        w_experts=np.ascontiguousarray(inputs["L0_ffn_w_experts"], np.float32),
        b_experts=np.ascontiguousarray(inputs["L0_ffn_b_experts"], np.float32),
        b_exp_bcast=np.ascontiguousarray(np.broadcast_to(
            np.asarray(inputs["L0_ffn_b_experts"], np.float32)[:, None, :], (8, 128, 1024))),
        w_out_ffn=np.ascontiguousarray(inputs["L0_ffn_w_out"], np.float32),
        b_out=np.ascontiguousarray(inputs["L0_ffn_b_out"], np.float32),
        ident=np.asarray(ident),
    )
    in_maps = []
    for c in range(8):
        bb, hh = c % 4, c // 4
        xs = np.ascontiguousarray(x[bb, hh * NT:(hh + 1) * NT, :], np.float32)
        sel = np.zeros((8, 1), np.float32)
        if hh == 1:
            sel[bb, 0] = 1.0
        m = dict(common)
        m["x"] = xs
        m["hsel"] = sel
        in_maps.append(m)
    res = run_bass_kernel_spmd(nc, in_maps, core_ids=list(range(8)))
    out = np.empty((4, 8192, 1024), np.float32)
    for c in range(8):
        bb, hh = c % 4, c // 4
        out[bb, hh * NT:(hh + 1) * NT, :] = res.results[c]["out"]
    return out
